# version 28
# baseline (speedup 1.0000x reference)
import numpy as np
import ml_dtypes
import concourse.bass as bass
import concourse.mybir as mybir
from concourse.bass_utils import run_bass_kernel_spmd
from contextlib import ExitStack

F32 = mybir.dt.float32
BF16 = mybir.dt.bfloat16
AF = mybir.ActivationFunctionType
ALU = mybir.AluOpType
AX = mybir.AxisListType.X
BF = ml_dtypes.bfloat16

T = 4096
E = 1024
H = 16
D = 64
NQT = 32
NEG = -1e9
NHSB = 7

EGs = [qi // 8 + 1 for qi in range(NQT)]
TGs = [qi // 4 + 1 for qi in range(NQT)]
Ks = [128 * (qi + 1) for qi in range(NQT)]

def _cum(xs):
    out = [0]
    for x in xs:
        out.append(out[-1] + x)
    return out

cum_scg = _cum([16 * e for e in EGs])
cum_tg = _cum(TGs)
cum_tc = _cum([tg + 1 for tg in TGs])
cum_ch = _cum([qi + 1 for qi in range(NQT)])

e1_of_flat = []
e1_zred_wait = {}
_c1 = 0
for _qi in range(NQT):
    for _h in range(H):
        for _g in range(EGs[_qi]):
            _c1 += 1
            e1_of_flat.append(_c1)
        _c1 += 1
        e1_zred_wait[(_qi, _h)] = _c1
E1_TOTAL = _c1


ln_val = {}
_ct = 0
for _qi in range(NQT + 2):
    if 0 <= _qi - 2 < NQT:
        _ct += 1
        ln_val[("ln2", _qi - 2)] = _ct
    if 0 <= _qi - 1 < NQT:
        _ct += 1
        ln_val[("ln1", _qi - 1)] = _ct

aln_val = {}
for _qi in range(NQT):
    aln_val[("ln1", _qi)] = 2 * _qi + 1
    aln_val[("ln2", _qi)] = 2 * _qi + 2

e2cum = _cum([(tg + 1) // 2 + 1 for tg in TGs])

G_TOTAL = cum_scg[-1]
TG_TOTAL = cum_tg[-1]


def build_nc():
    nc = bass.Bass("TRN2", target_bir_lowering=False)

    hs = nc.dram_tensor("hs", [T, E], F32, kind="ExternalInput").ap()
    g2 = nc.dram_tensor("g2", [T, T], F32, kind="ExternalInput").ap()
    wqT = nc.dram_tensor("wqT", [E, E], F32, kind="ExternalInput").ap()
    wkT = nc.dram_tensor("wkT", [E, E], F32, kind="ExternalInput").ap()
    bqt = nc.dram_tensor("bqt", [128, 8], F32, kind="ExternalInput").ap()
    bkt = nc.dram_tensor("bkt", [128, 8], F32, kind="ExternalInput").ap()
    wqfT = nc.dram_tensor("wqfT", [E, E], F32, kind="ExternalInput").ap()
    rT = nc.dram_tensor("rT", [E, 255], F32, kind="ExternalInput").ap()
    lv = nc.dram_tensor("lv", [256, E], F32, kind="ExternalInput").ap()
    woT = nc.dram_tensor("woT", [E, E], F32, kind="ExternalInput").ap()
    wob = nc.dram_tensor("wob", [1, E], F32, kind="ExternalInput").ap()
    smat = nc.dram_tensor("smat", [512, 256], F32, kind="ExternalInput").ap()
    idb = nc.dram_tensor("idb", [128, 128], BF16, kind="ExternalInput").ap()
    idf = nc.dram_tensor("idf", [128, 128], F32, kind="ExternalInput").ap()
    maskneg = nc.dram_tensor("maskneg", [128, 128], BF16, kind="ExternalInput").ap()

    hs_bf = nc.dram_tensor("hs_bf", [T, E], BF16, kind="Internal").ap()
    q_sc = nc.dram_tensor("q_sc", [T, E], BF16, kind="Internal").ap()

    y = nc.dram_tensor("y", [T, E], F32, kind="ExternalOutput").ap()

    with ExitStack() as ctx:
        sb = lambda n, s, d: ctx.enter_context(nc.sbuf_tensor(n, s, d))
        psa = lambda n, s, d: ctx.enter_context(nc.psum_tensor(n, s, d))
        sem = lambda n: ctx.enter_context(nc.semaphore(n))

        k_fm = sb("k_fm", [128, 8 * T], BF16)
        arena = sb("arena", [128, 20480], BF16)
        w_s = sb("w_s", [128, 4 * 256], F32)
        w_wob_bf = sb("w_wob_bf", [1, E], BF16)
        s_idb = sb("s_idb", [128, 128], BF16)
        s_idf = sb("s_idf", [128, 128], F32)
        s_mask = sb("s_mask", [128, 128], BF16)
        s_bqt = sb("s_bqt", [128, 8], F32)
        s_bkt = sb("s_bkt", [128, 8], F32)
        s_ones1 = sb("s_ones1", [1, 128], BF16)
        s_onesb = sb("s_onesb", [128, 1], F32)
        s_eps = sb("s_eps", [128, 1], F32)
        jnka = sb("jnka", [128, 1], F32)
        jnkv = sb("jnkv", [128, 1], F32)
        jnkp = sb("jnkp", [128, 1], F32)

        E0 = sb("E0", [128, T], BF16)
        E1 = sb("E1", [128, T], BF16)
        Ebufs = [E0, E1]
        avg = sb("avg", [128, T], F32)
        dw = sb("dw", [128, T], BF16)
        g2q = [sb(f"g2q{i}", [128, 512], F32) for i in range(2)]
        qf = [sb(f"qf{i}", [128, E], BF16) for i in range(2)]
        x0 = [sb(f"x0{i}", [128, E], F32) for i in range(2)]
        hsb = [sb(f"hsb{i}", [128, E], BF16) for i in range(NHSB)]
        dwT = [sb(f"dwT{i}", [128, 512], BF16) for i in range(2)]
        Zp = sb("Zp", [128, 64], F32)
        Zt = sb("Zt", [128, 2], F32)
        c_h = sb("c_h", [128, 2], F32)
        Z2 = sb("Z2", [128, 1], F32)
        Z2p = sb("Z2p", [128, 8], F32)
        rZ2 = sb("rZ2", [128, 1], F32)
        s1 = sb("s1", [128, E], F32)
        x1 = sb("x1", [128, E], F32)
        msum = sb("msum", [128, 1], F32)
        m2 = sb("m2", [128, 1], F32)
        negm2 = sb("negm2", [128, 1], F32)
        msq = sb("msq", [128, 1], F32)
        ssq = sb("ssq", [128, 1], F32)
        lnv = sb("lnv", [128, 1], F32)
        rstd = sb("rstd", [128, 1], F32)
        x1f = sb("x1f", [128, E], BF16)
        qff = x1f
        esP = sb("esP", [128, 256], F32)
        esM = sb("esM", [128, 256], F32)
        lg = sb("lg", [128, 512], F32)
        lp = sb("lp", [128, 256], BF16)
        retf = sb("retf", [128, E], BF16)
        yt = None
        w_wob = avg[0:1, 3072:4096]
        wstage = [sb("wstage0", [128, 1024], F32), avg[:, 2048:3072]]
        pro_hs = [avg[:, 0:1024], avg[:, 1024:2048]]
        pro_hsb = [arena[:, 16384:17408], arena[:, 17408:18432]]
        hfm = [arena[:, 18432:19456], arena[:, 19456:20480]]
        q_pro = [E0[:, 0:1024], E0[:, 1024:2048]]

        scps = [psa("p_scA", [128, 1024], F32), psa("p_scB", [128, 1024], F32)]
        big = psa("p_big", [128, 1024], F32)
        trps = [psa("p_trA", [128, 1024], BF16), psa("p_trB", [128, 1024], BF16)]
        PREG = [big, scps[0], scps[1]]

        s_wconst = sem("s_wconst")
        s_wst = sem("s_wst")
        s_ph = sem("s_ph")
        s_hsw = sem("s_hsw")
        s_qsw = sem("s_qsw")
        s_ystr = sem("s_ystr")
        s_qfs = [sem("s_qf0"), sem("s_qf1")]
        s_x0s = [sem("s_x00"), sem("s_x01")]
        s_g2s = [sem(f"s_g2{i}") for i in range(7)]
        s_mev = sem("s_mev")
        s_hsbs = [sem(f"s_hsb{i}") for i in range(NHSB)]
        s_dvec = sem("s_dvec")
        s_poolc = sem("s_poolc")
        s_ppro = sem("s_ppro")
        s_apro = sem("s_apro")
        s_sc = sem("s_sc")
        s_e1 = sem("s_e1")
        s_stt = sem("s_stt")
        s_tc = sem("s_tc")
        s_e2 = sem("s_e2")
        s_ptr2 = sem("s_ptr2")
        s_atrev = sem("s_atrev")
        s_mix = sem("s_mix")
        s_s1 = sem("s_s1")
        s_ln = sem("s_ln")
        s_aln = sem("s_aln")
        s_x1 = sem("s_x1")
        s_ffnp = sem("s_ffnp")
        s_ffne = sem("s_ffne")
        s_y = sem("s_y")
        s_zr = sem("s_zr")
        s_rc = sem("s_rc")
        s_rz = sem("s_rz")
        s_ar = sem("s_ar")
        s_fw = [sem("s_fw0"), sem("s_fw1")]
        s_wbf = sem("s_wbf")

        def ffn_slot(i):
            if i < 8:
                return i * 1024, 1024
            if i < 16:
                return 8192 + (i - 8) * 256, 255
            if i < 18:
                return 10240 + (i - 16) * 1024, 1024
            return 12288 + (i - 18) * 1024, 1024

        def ffn_src(i):
            if i < 8:
                return wqfT, i * 128
            if i < 16:
                return rT, (i - 8) * 128
            if i < 18:
                return lv, (i - 16) * 128
            return woT, (i - 18) * 128

        chunk_consumer = {}
        for qi in range(NQT):
            for j in range(qi + 1):
                chunk_consumer[cum_ch[qi] + j] = cum_tg[qi] + j // 4

        g2_load_war = {}
        g2_cnt = {}
        _g2_last = {}
        _g2_n = [0] * 7
        for qi in range(NQT):
            for c in range(TGs[qi]):
                b = c % 7
                waits = []
                if c >= 7:
                    waits.append(("tc", cum_tc[qi] + (c - 7) + 1))
                elif b in (5, 6):
                    waits.append(("mev", qi))
                else:
                    if b == 2 and qi >= 2:
                        waits.append(("ffnp", 7 * (qi - 2) + 5))
                    if b in (3, 4) and qi >= 1:
                        waits.append(("s1", qi - 1))
                    if b in _g2_last:
                        pq, pc = _g2_last[b]
                        waits.append(("tc", cum_tc[pq] + pc + 1))
                _g2_last[b] = (qi, c)
                _g2_n[b] += 1
                g2_cnt[(qi, c)] = _g2_n[b]
                g2_load_war[(qi, c)] = waits

        def g2buf(qi, c):
            b = c % 7
            if b < 2:
                return g2q[b]
            if b == 2:
                return lg[:, 0:512]
            if b in (3, 4):
                return x1[:, (b - 3) * 512:(b - 2) * 512]
            return x0[(qi + 1) % 2][:, (b - 5) * 512:(b - 4) * 512]

        last_wait = {}

        def W(eng, key, semh, val):
            if val <= 0:
                return
            k = (key, id(semh))
            if last_wait.get(k, -1) >= val:
                return
            last_wait[k] = val
            eng.wait_ge(semh, val)

        block = ctx.enter_context(nc.Block())

        @block.gpsimd
        def _(g):
            g.dma_start(out=s_idb[:], in_=idb).then_inc(s_wconst, 16)
            g.dma_start(out=s_idf[:], in_=idf).then_inc(s_wconst, 16)
            g.dma_start(out=s_mask[:], in_=maskneg).then_inc(s_wconst, 16)
            g.dma_start(out=s_bqt[:], in_=bqt).then_inc(s_wconst, 16)
            g.dma_start(out=s_bkt[:], in_=bkt).then_inc(s_wconst, 16)
            g.dma_start(out=w_wob[:], in_=wob).then_inc(s_wconst, 16)
            for i in range(4):
                g.dma_start(out=w_s[:, i * 256:(i + 1) * 256],
                            in_=smat[i * 128:(i + 1) * 128, :]).then_inc(s_wconst, 16)

            def _wcast(i):
                kc = i % 8
                base = kc * 1024 if i < 8 else 8192 + kc * 1024
                W(g, "g", s_wst, 16 * (i + 1))
                g.tensor_copy(out=arena[:, base:base + 1024],
                              in_=wstage[0][:]).then_inc(s_poolc, 1)

            for i in range(16):
                src_t = wqT if i < 8 else wkT
                kc = i % 8
                g.dma_start(out=wstage[0][:],
                            in_=src_t[kc * 128:(kc + 1) * 128, :]).then_inc(s_wst, 16)
                _wcast(i)
            for tc in range(2):
                W(g, "g", s_ph, 16 * tc)
                g.dma_start(out=pro_hs[tc % 2][:],
                            in_=hs[tc * 128:(tc + 1) * 128, :]).then_inc(s_ph, 16)
            for tc in range(NQT):
                W(g, "g", s_dvec, tc + 1)
                W(g, "g", s_hsw, 16 * tc)
                g.dma_start(out=hs_bf[tc * 128:(tc + 1) * 128, :],
                            in_=pro_hsb[tc % 2][:]).then_inc(s_hsw, 16)
                W(g, "g", s_apro, 18 * tc + 10)
                W(g, "g", s_qsw, 16 * tc)
                g.dma_start(out=q_sc[tc * 128:(tc + 1) * 128, :],
                            in_=q_pro[tc % 2][:]).then_inc(s_qsw, 16)
                if tc + 2 < NQT:
                    W(g, "g", s_ph, 16 * (tc + 2))
                    g.dma_start(out=pro_hs[tc % 2][:],
                                in_=hs[(tc + 2) * 128:(tc + 3) * 128, :]).then_inc(s_ph, 16)
            for qi in range(NQT):
                W(g, "g", s_y, qi + 1)
                W(g, "g", s_ystr, 16 * qi)
                g.dma_start(out=y[qi * 128:(qi + 1) * 128, :],
                            in_=x0[qi % 2][:]).then_inc(s_ystr, 16)
            g.wait_ge(s_ystr, 16 * NQT)
            g.wait_ge(s_hsw, 16 * NQT)
            g.wait_ge(s_qsw, 16 * NQT)
            g.wait_ge(s_wst, 16 * 16)
            g.wait_ge(s_wconst, 16 * 10)
            g.wait_ge(s_ph, 16 * NQT)

        @block.sync
        def _(s):
            for qi in range(NQT):
                if qi == 1:
                    W(s, "s", s_ppro, 3 * NQT)
                    for i in range(26):
                        src_t, r0 = ffn_src(i)
                        ncols = 255 if (8 <= i < 16) else 1024
                        if i >= 2:
                            W(s, "s", s_poolc, 16 + (i - 1))
                        s.dma_start(out=wstage[i % 2][:, 0:ncols],
                                    in_=src_t[r0:r0 + 128, 0:ncols]
                                    ).then_inc(s_fw[i % 2], 16)
                for c in range(TGs[qi]):
                    for kind, val in g2_load_war[(qi, c)]:
                        W(s, "s", {"tc": s_tc, "mev": s_mev, "s1": s_s1,
                                   "ffnp": s_ffnp}[kind], val)
                    c0 = c * 512
                    c1 = min(Ks[qi], c0 + 512)
                    s.dma_start(out=g2buf(qi, c)[:, 0:c1 - c0],
                                in_=g2[qi * 128:(qi + 1) * 128, c0:c1]
                                ).then_inc(s_g2s[c % 7], 16)
                for j in range(qi + 1):
                    gm = cum_ch[qi] + j
                    W(s, "s", s_hsw, 16 * (j + 1))
                    if gm >= NHSB:
                        W(s, "s", s_mix, chunk_consumer[gm - NHSB] + 1)
                    s.dma_start(out=hsb[gm % NHSB][:],
                                in_=hs_bf[j * 128:(j + 1) * 128, :]
                                ).then_inc(s_hsbs[gm % NHSB], 16)

        @block.vector
        def _(v):
            v.memset(s_ones1[:], 1.0)
            v.memset(s_onesb[:], 1.0)
            v.memset(s_eps[:], 1e-5)
            v.memset(esP[:, 255:256], 0.0)
            v.memset(esM[:, 255:256], 0.0)
            W(v, "v", s_wconst, 16 * 10)
            v.tensor_copy(out=w_wob_bf[:], in_=w_wob[:]).then_inc(s_wbf, 1)
            for tc in range(NQT):
                W(v, "v", s_ph, 16 * (tc + 1))
                if tc >= 2:
                    W(v, "v", s_ppro, 3 * (tc - 2) + 1)
                    W(v, "v", s_hsw, 16 * (tc - 1))
                v.tensor_copy(out=pro_hsb[tc % 2][:], in_=pro_hs[tc % 2][:]
                              ).then_inc(s_dvec, 1)
            zr_cnt = [0]
            rz_cnt = [0]
            fc_idx = [0]

            def dve_fc(n=2):
                for _ in range(n):
                    i = fc_idx[0]
                    if i >= 26:
                        return
                    fc_idx[0] += 1
                    if i == 18:
                        W(v, "v", s_hsw, 16 * NQT)
                    base, ncols = ffn_slot(i)
                    W(v, "v", s_fw[i % 2], 16 * (i // 2 + 1))
                    v.tensor_copy(out=arena[:, base:base + ncols],
                                  in_=wstage[i % 2][:, 0:ncols]
                                  ).then_inc(s_poolc, 1)

            def st_recip(qi, h):
                if EGs[qi] == 1:
                    srcz = Zp[:, h * 4:h * 4 + 1]
                else:
                    srcz = Zt[:, h % 2:h % 2 + 1]
                v.reciprocal(out=c_h[:, h % 2:h % 2 + 1], in_=srcz
                             ).then_inc(s_rc, 1)

            def st_stt(qi, h):
                K = Ks[qi]
                cs = c_h[:, h % 2:h % 2 + 1]
                W(v, "v", s_rc, 16 * qi + h + 1)
                if h == 0:
                    W(v, "v", s_e2, e2cum[qi])
                    v.tensor_scalar(out=avg[:, 0:K], in0=Ebufs[h % 2][:, 0:K],
                                    scalar1=cs, scalar2=None,
                                    op0=ALU.mult).then_inc(s_stt, 1)
                else:
                    v.scalar_tensor_tensor(out=avg[:, 0:K],
                                           in0=Ebufs[h % 2][:, 0:K],
                                           scalar=cs, in1=avg[:, 0:K],
                                           op0=ALU.mult, op1=ALU.add
                                           ).then_inc(s_stt, 1)

            def dve_head_seq(qi, h):
                W(v, "v", s_e1, e1_zred_wait[(qi, h)])
                if qi == 0:
                    v.memset(Zp[0:1, h * 4:h * 4 + 1], 1.0)
                    v.drain()
                st_recip(qi, h)
                v.drain()
                st_stt(qi, h)

            def dve_tcomb(qi, crange):
                K = Ks[qi]
                for c in crange:
                    c0 = c * 512
                    c1 = min(K, c0 + 512)
                    W(v, "v", s_g2s[c % 7], 16 * g2_cnt[(qi, c)])
                    v.scalar_tensor_tensor(out=avg[:, c0:c1], in0=avg[:, c0:c1],
                                           scalar=1.0 / 32.0,
                                           in1=g2buf(qi, c)[:, 0:c1 - c0],
                                           op0=ALU.mult, op1=ALU.add).then_inc(s_tc, 1)
                if crange and crange[-1] == TGs[qi] - 1:
                    v.tensor_tensor(out=avg[:, K - 128:K], in0=avg[:, K - 128:K],
                                    in1=s_mask[:], op=ALU.add).then_inc(s_tc, 1)

            def dve_z2chain(qi):
                W(v, "v", s_e2, e2cum[qi + 1])
                rz_cnt[0] += 1
                v.tensor_reduce(out=Z2[:], in_=Z2p[:, 0:(TGs[qi] + 1) // 2],
                                axis=AX, op=ALU.add).then_inc(s_rz, 1)
                if qi == 0:
                    v.drain()
                    v.memset(Z2[0:1, 0:1], 1.0)
                    v.drain()
                W(v, "v", s_rz, rz_cnt[0])
                rz_cnt[0] += 1
                v.reciprocal(out=rZ2[:], in_=Z2[:]).then_inc(s_rz, 1)

            def dve_mix(qi):
                W(v, "v", s_mix, cum_tg[qi + 1])
                W(v, "v", s_x0s[qi % 2], 16 * (qi // 2 + 1))
                W(v, "v", s_rz, 2 * (qi + 1))
                v.scalar_tensor_tensor(out=s1[:], in0=big[:, 0:E], scalar=rZ2[:],
                                       in1=x0[qi % 2][:], op0=ALU.mult,
                                       op1=ALU.add, accum_out=msum[:]
                                       ).then_inc(s_mev, 1)

            def _ln_core(qi, which, sqdst, outdst, outsem):
                v.scalar_tensor_tensor(out=sqdst[:], in0=s1[:], scalar=1.0,
                                       in1=s1[:], op0=ALU.mult, op1=ALU.mult,
                                       accum_out=ssq[:])
                v.scalar_tensor_tensor(out=negm2[:], in0=msum[:],
                                       scalar=-1.0 / (1024.0 * 1024.0),
                                       in1=msum[:], op0=ALU.mult, op1=ALU.mult)
                v.tensor_scalar(out=m2[:], in0=msum[:], scalar1=1.0 / 1024.0,
                                scalar2=None, op0=ALU.mult)
                v.tensor_scalar(out=msq[:], in0=negm2[:], scalar1=1e-5,
                                scalar2=None, op0=ALU.add).then_inc(s_ln, 1)
                W(v, "v", s_aln, aln_val[(which, qi)])
                v.tensor_scalar(out=outdst[:], in0=s1[:], scalar1=m2[:],
                                scalar2=rstd[:], op0=ALU.subtract,
                                op1=ALU.mult).then_inc(outsem, 1)

            def dve_ln1a(qi):
                if qi >= 1:
                    W(v, "v", s_ffnp, 7 * (qi - 1) + 3)
                v.scalar_tensor_tensor(out=qff[:], in0=s1[:], scalar=1.0,
                                       in1=s1[:], op0=ALU.mult, op1=ALU.mult,
                                       accum_out=ssq[:])
                v.scalar_tensor_tensor(out=negm2[:], in0=msum[:],
                                       scalar=-1.0 / (1024.0 * 1024.0),
                                       in1=msum[:], op0=ALU.mult, op1=ALU.mult)
                v.tensor_scalar(out=m2[:], in0=msum[:], scalar1=1.0 / 1024.0,
                                scalar2=None, op0=ALU.mult)
                v.tensor_scalar(out=msq[:], in0=negm2[:], scalar1=1e-5,
                                scalar2=None, op0=ALU.add).then_inc(s_ln, 1)

            def dve_ln1b(qi):
                W(v, "v", s_aln, aln_val[("ln1", qi)])
                v.tensor_scalar(out=x1[:], in0=s1[:], scalar1=m2[:],
                                scalar2=rstd[:], op0=ALU.subtract,
                                op1=ALU.mult).then_inc(s_x1, 1)

            def dve_res2(qi):
                W(v, "v", s_ffnp, 7 * qi + 7)
                v.scalar_tensor_tensor(out=s1[:], in0=big[:, 0:E], scalar=1.0,
                                       in1=x1[:], op0=ALU.mult, op1=ALU.add,
                                       accum_out=msum[:]).then_inc(s_s1, 1)

            def dve_ln2(qi):
                _ln_core(qi, "ln2", x1f, x0[qi % 2], s_y)

            for qi in range(NQT + 2):
                if qi == 0:
                    for h in range(H):
                        dve_head_seq(qi, h)
                        dve_fc()
                elif qi < NQT:
                    EG = EGs[qi]
                    for h in range(H):
                        if h >= 1:
                            st_stt(qi, h - 1)
                        if h == 1 and 0 <= qi - 2 < NQT:
                            dve_ln1a(qi - 2)
                        if h == 2 and 0 <= qi - 2 < NQT:
                            dve_ln1b(qi - 2)
                        if h == 10 and 0 <= qi - 2 < NQT:
                            dve_res2(qi - 2)
                        if h == 13 and 0 <= qi - 2 < NQT:
                            dve_ln2(qi - 2)
                        if h == 14 and 0 <= qi - 1 < NQT:
                            dve_z2chain(qi - 1)
                        W(v, "v", s_e1, e1_zred_wait[(qi, h)])
                        if EG > 1:
                            zr_cnt[0] += 1
                            v.tensor_reduce(out=Zt[:, h % 2:h % 2 + 1],
                                            in_=Zp[:, h * 4:h * 4 + EG],
                                            axis=AX, op=ALU.add
                                            ).then_inc(s_zr, 1)
                            W(v, "v", s_zr, zr_cnt[0])
                        st_recip(qi, h)
                        if qi <= 3:
                            dve_fc()
                    st_stt(qi, H - 1)
                else:
                    if 0 <= qi - 2 < NQT:
                        dve_ln1a(qi - 2)
                        dve_ln1b(qi - 2)
                        dve_res2(qi - 2)
                        dve_ln2(qi - 2)
                    if 0 <= qi - 1 < NQT:
                        dve_z2chain(qi - 1)
                if qi < NQT:
                    dve_tcomb(qi, list(range(min(2, TGs[qi]))))
                if 0 <= qi - 1 < NQT:
                    dve_mix(qi - 1)
                if qi < NQT:
                    dve_tcomb(qi, list(range(2, TGs[qi])))

        @block.tensor
        def _(t):
            W(t, "t", s_wconst, 16 * 10)
            W(t, "t", s_poolc, 16)
            for tc in range(NQT):
                W(t, "t", s_dvec, tc + 1)
                W(t, "t", s_apro, 18 * (tc - 1) + 2)
                for j in range(8):
                    tp = trps[j // 4]
                    ins = t.transpose(tp[:, (j % 4) * 128:(j % 4) * 128 + 128],
                                      pro_hsb[tc % 2][:, j * 128:(j + 1) * 128],
                                      s_idb[:])
                    if j == 7:
                        ins.then_inc(s_ppro, 1)
                W(t, "t", s_apro, 18 * tc + 2)
                uq = 2 * tc
                if uq >= 3:
                    up = uq - 3
                    W(t, "t", s_apro,
                      18 * (up // 2) + (10 if up % 2 == 0 else 18))
                for eo in range(8):
                    for kc in range(8):
                        ins = t.matmul(PREG[uq % 3][:, eo * 128:(eo + 1) * 128],
                                       arena[:, kc * 1024 + eo * 128:kc * 1024 + eo * 128 + 128],
                                       hfm[tc % 2][:, kc * 128:(kc + 1) * 128],
                                       start=(kc == 0), stop=(kc == 7))
                        if eo == 7 and kc == 7:
                            ins.then_inc(s_ppro, 1)
                uk = 2 * tc + 1
                if uk >= 3:
                    up = uk - 3
                    W(t, "t", s_apro,
                      18 * (up // 2) + (10 if up % 2 == 0 else 18))
                for eo in range(8):
                    for kc in range(8):
                        ins = t.matmul(
                            PREG[uk % 3][:, eo * 128:(eo + 1) * 128],
                            arena[:, 8192 + kc * 1024 + eo * 128:8192 + kc * 1024 + eo * 128 + 128],
                            hfm[tc % 2][:, kc * 128:(kc + 1) * 128],
                            start=(kc == 0), stop=(kc == 7))
                        if eo == 7 and kc == 7:
                            ins.then_inc(s_ppro, 1)
            def pe_sc_head(qi, h, pieces=()):
                K = Ks[qi]
                EG = EGs[qi]
                pi = 0
                for gloc in range(EG):
                    g = cum_scg[qi] + h * EG + gloc
                    if qi == 0 and h == 0 and gloc == 0:
                        W(t, "t", s_apro, 18 * NQT)
                    if gloc == 0:
                        W(t, "t", s_qfs[qi % 2], 16 * (qi // 2 + 1))
                    if g >= 2:
                        W(t, "t", s_e1, e1_of_flat[g - 2])
                    ps = scps[g % 2]
                    c0 = gloc * 1024
                    c1 = min(K, c0 + 1024)
                    p0 = (h & 1) * 64
                    lhs = qf[qi % 2][p0:p0 + 64, (h >> 1) * 128:(h >> 1) * 128 + 128]
                    base = (h >> 1) * T
                    nmm = ((c1 - c0) + 511) // 512
                    isdiag = (gloc == EG - 1)
                    for mmi in range(nmm):
                        a = c0 + mmi * 512
                        b = min(c1, a + 512)
                        last = (mmi == nmm - 1)
                        ins = t.matmul(ps[:, a - c0:b - c0], lhs,
                                       k_fm[p0:p0 + 64, base + a:base + b],
                                       start=True, stop=not (isdiag and last))
                        if last and not isdiag:
                            ins.then_inc(s_sc, 1)
                    if isdiag:
                        t.matmul(ps[:, K - 128 - c0:K - c0], s_idb[:],
                                 s_mask[:], start=False, stop=True
                                 ).then_inc(s_sc, 1)
                    if pi < len(pieces):
                        pieces[pi]()
                        pi += 1
                while pi < len(pieces):
                    pieces[pi]()
                    pi += 1

            def _trhalf(qi, tg, half):
                gtg = cum_tg[qi] + tg
                nch = min(4, (qi + 1) - tg * 4)
                i0 = 0 if half == 0 else (nch + 1) // 2
                i1 = (nch + 1) // 2 if half == 0 else nch
                if i0 >= i1:
                    return
                W(t, "t", s_e2, e2cum[qi] + tg // 2 + 1)
                W(t, "t", s_atrev, gtg - 1)
                for ii in range(i0, i1):
                    j = tg * 4 + ii
                    ins = t.transpose(trps[gtg % 2][:, ii * 128:(ii + 1) * 128],
                                      dw[:, j * 128:(j + 1) * 128], s_idb[:])
                    if ii == nch - 1:
                        ins.then_inc(s_ptr2, 1)

            def _mixchunk(qi, tg, ii):
                gtg = cum_tg[qi] + tg
                nch = min(4, (qi + 1) - tg * 4)
                W(t, "t", s_atrev, gtg + 1)
                if tg == 0 and ii == 0:
                    W(t, "t", s_s1, qi)
                j = tg * 4 + ii
                gm = cum_ch[qi] + j
                W(t, "t", s_hsbs[gm % NHSB], 16 * (gm // NHSB + 1))
                for half in range(2):
                    ins = t.matmul(big[:, half * 512:(half + 1) * 512],
                                   dwT[gtg % 2][:, ii * 128:(ii + 1) * 128],
                                   hsb[gm % NHSB][:, half * 512:(half + 1) * 512],
                                   start=(j == 0), stop=(j == qi))
                    if ii == nch - 1 and half == 1:
                        ins.then_inc(s_mix, 1)

            def _ffn1(qi, half):
                W(t, "t", s_x1, qi + 1)
                for j in range(half * 4, half * 4 + 4):
                    ins = t.transpose(big[:, j * 128:(j + 1) * 128],
                                      x1[:, j * 128:(j + 1) * 128], s_idf[:])
                    if j == 7:
                        ins.then_inc(s_ffnp, 1)

            def _ffn2(qi, quarter):
                W(t, "t", s_ffne, 7 * qi + 1)
                if qi == 0:
                    W(t, "t", s_poolc, 16 + 8)
                for eo in range(quarter * 2, quarter * 2 + 2):
                    for kc in range(8):
                        ins = t.matmul(big[:, eo * 128:(eo + 1) * 128],
                                       arena[:, kc * 1024 + eo * 128:kc * 1024 + eo * 128 + 128],
                                       x1f[:, kc * 128:(kc + 1) * 128],
                                       start=(kc == 0), stop=(kc == 7))
                        if eo == 7 and kc == 7:
                            ins.then_inc(s_ffnp, 1)

            def _ffn3(qi, half):
                W(t, "t", s_ffne, 7 * qi + 2)
                if qi == 0:
                    W(t, "t", s_poolc, 16 + 16)
                for kc in range(half * 4, half * 4 + 4):
                    ins = t.matmul(big[:, 0:255],
                                   qff[:, kc * 128:(kc + 1) * 128],
                                   arena[:, 8192 + kc * 256:8192 + kc * 256 + 255],
                                   start=(kc == 0), stop=(kc == 7))
                    if kc == 7:
                        ins.then_inc(s_ffnp, 1)

            def _ffn4(qi):
                W(t, "t", s_ffne, 7 * qi + 4)
                for j in range(4):
                    srct = esP if j < 2 else esM
                    ins = t.transpose(big[:, j * 128:(j + 1) * 128],
                                      srct[:, (j % 2) * 128:(j % 2) * 128 + 128],
                                      s_idf[:])
                    if j == 3:
                        ins.then_inc(s_ffnp, 1)

            def _ffn5(qi):
                W(t, "t", s_ffne, 7 * qi + 5)
                for lc in range(2):
                    for kc in range(4):
                        ins = t.matmul(big[:, lc * 128:(lc + 1) * 128],
                                       w_s[:, kc * 256 + lc * 128:kc * 256 + lc * 128 + 128],
                                       lg[:, kc * 128:(kc + 1) * 128],
                                       start=(kc == 0), stop=(kc == 3))
                        if lc == 1 and kc == 3:
                            ins.then_inc(s_ffnp, 1)

            def _ffn6(qi, half):
                W(t, "t", s_ffne, 7 * qi + 6)
                if qi == 0:
                    W(t, "t", s_poolc, 16 + 18)
                for dj in range(half * 4, half * 4 + 4):
                    for lc in range(2):
                        ins = t.matmul(big[:, dj * 128:(dj + 1) * 128],
                                       arena[:, 10240 + lc * 1024 + dj * 128:10240 + lc * 1024 + dj * 128 + 128],
                                       lp[:, lc * 128:(lc + 1) * 128],
                                       start=(lc == 0), stop=(lc == 1))
                        if dj == 7 and lc == 1:
                            ins.then_inc(s_ffnp, 1)

            def _ffn7(qi, half):
                W(t, "t", s_ffne, 7 * qi + 7)
                if qi == 0:
                    W(t, "t", s_poolc, 16 + 26)
                    W(t, "t", s_wbf, 1)
                for kc in range(8):
                    t.matmul(big[:, half * 512:(half + 1) * 512],
                             retf[:, kc * 128:(kc + 1) * 128],
                             arena[:, 12288 + kc * 1024 + half * 512:12288 + kc * 1024 + half * 512 + 512],
                             start=(kc == 0), stop=False)
                ins = t.matmul(big[:, half * 512:(half + 1) * 512],
                               s_ones1[:], w_wob_bf[:, half * 512:(half + 1) * 512],
                               start=False, stop=True)
                if half == 1:
                    ins.then_inc(s_ffnp, 1)

            for qi in range(NQT + 2):
                qT = qi - 1
                qF = qi - 2
                TM = []
                if 0 <= qT < NQT:
                    def _trps(tg):
                        return [lambda hh=hh, tg=tg: _trhalf(qT, tg, hh)
                                for hh in range(2)]

                    def _mxps(tg):
                        nch = min(4, (qT + 1) - tg * 4)
                        return [lambda ii=ii, tg=tg: _mixchunk(qT, tg, ii)
                                for ii in range(nch)]
                    for tg in range(TGs[qT]):
                        TM.append(_trps(tg))
                        if tg >= 1:
                            TM.append(_mxps(tg - 1))
                    TM.append(_mxps(TGs[qT] - 1))
                if 0 <= qF < NQT:
                    Fq = [
                        [lambda hh=hh: _ffn1(qF, hh) for hh in range(2)],
                        [lambda qq=qq: _ffn2(qF, qq) for qq in range(4)],
                        [lambda hh=hh: _ffn3(qF, hh) for hh in range(2)],
                        [lambda: _ffn4(qF)],
                        [lambda: _ffn5(qF)],
                        [lambda hh=hh: _ffn6(qF, hh) for hh in range(2)],
                        [lambda hh=hh: _ffn7(qF, hh) for hh in range(2)],
                    ]
                else:
                    Fq = []
                ut = uf = 0
                pending = []
                for h in range(H):
                    give, pending = pending[:2], pending[2:]
                    if qi < NQT:
                        pe_sc_head(qi, h, give)
                    else:
                        for p in give:
                            p()
                    if 2 <= h <= 8 and uf < len(Fq):
                        pending += Fq[uf]
                        uf += 1
                    if h >= 9:
                        for _ in range(2):
                            if ut < len(TM):
                                for p in TM[ut]:
                                    p()
                                ut += 1
                for p in pending:
                    p()
                while uf < len(Fq):
                    for p in Fq[uf]:
                        p()
                    uf += 1
                while ut < len(TM):
                    for p in TM[ut]:
                        p()
                    ut += 1

        @block.scalar
        def _(a):
            for tc in range(NQT):
                W(a, "a", s_ppro, 3 * tc + 1)
                a.copy(out=hfm[tc % 2][:, 0:512], in_=trps[0][:, 0:512]
                       ).then_inc(s_apro, 1)
                a.copy(out=hfm[tc % 2][:, 512:1024], in_=trps[1][:, 0:512]
                       ).then_inc(s_apro, 1)
                W(a, "a", s_ppro, 3 * tc + 2)
                if tc >= 2:
                    W(a, "a", s_qsw, 16 * (tc - 1))
                for eo in range(8):
                    ins = a.activation(out=q_pro[tc % 2][:, eo * 128:(eo + 1) * 128],
                                       in_=PREG[(2 * tc) % 3][:, eo * 128:(eo + 1) * 128],
                                       func=AF.Identity,
                                       bias=s_bqt[:, eo:eo + 1])
                    ins.then_inc(s_apro, 1)
                W(a, "a", s_ppro, 3 * tc + 3)
                for eo in range(8):
                    ins = a.activation(out=k_fm[:, eo * T + tc * 128:eo * T + tc * 128 + 128],
                                       in_=PREG[(2 * tc + 1) % 3][:, eo * 128:(eo + 1) * 128],
                                       func=AF.Identity,
                                       bias=s_bkt[:, eo:eo + 1])
                    ins.then_inc(s_apro, 1)
            W(a, "a", s_qsw, 16 * NQT)

            def act_exp2(qi):
                K = Ks[qi]
                W(a, "a", s_ptr2, cum_tg[qi])
                W(a, "a", s_rz, 2 * qi - 1)
                nch = (TGs[qi] + 1) // 2
                for c in range(nch):
                    c0 = c * 1024
                    c1 = min(K, c0 + 1024)
                    last = (c == nch - 1)
                    W(a, "a", s_tc,
                      cum_tc[qi + 1] if last else cum_tc[qi] + 2 * c + 2)
                    a.activation(out=dw[:, c0:c1], in_=avg[:, c0:c1],
                                 func=AF.Exp, accum_out=Z2p[:, c:c + 1]
                                 ).then_inc(s_e2, 1)
                a.copy(out=jnka[:], in_=s_eps[:]).then_inc(s_e2, 1)

            def act_head(qi, h):
                K = Ks[qi]
                EG = EGs[qi]
                nh = 16 * qi + h
                W(a, "a", s_stt, nh - 1)
                for gloc in range(EG):
                    g = cum_scg[qi] + h * EG + gloc
                    c0 = gloc * 1024
                    c1 = min(K, c0 + 1024)
                    W(a, "a", s_sc, g + 1)
                    ins = a.activation(out=Ebufs[h % 2][:, c0:c1],
                                       in_=scps[g % 2][:, 0:c1 - c0],
                                       func=AF.Exp, scale=0.125,
                                       accum_out=Zp[:, h * 4 + gloc:h * 4 + gloc + 1])
                    ins.then_inc(s_e1, 1)
                a.copy(out=jnka[:], in_=s_eps[:]).then_inc(s_e1, 1)

            def _tre(qi, tg):
                gtg = cum_tg[qi] + tg
                nch = min(4, (qi + 1) - tg * 4)
                W(a, "a", s_ptr2, gtg + 1)
                W(a, "a", s_mix, gtg - 1)
                a.copy(out=dwT[gtg % 2][:, 0:nch * 128],
                       in_=trps[gtg % 2][:, 0:nch * 128]).then_inc(s_atrev, 1)

            rstd_cnt = [0]

            def _rstd(qi, which):
                W(a, "a", s_ln, ln_val[("ln1" if which == 1 else "ln2", qi)])
                a.activation(out=lnv[:], in_=ssq[:], func=AF.Ln,
                             bias=msq[:], scale=1.0 / 1024.0).then_inc(s_ar, 1)
                rstd_cnt[0] += 1
                W(a, "a", s_ar, rstd_cnt[0])
                a.activation(out=rstd[:], in_=lnv[:], func=AF.Exp, scale=-0.5
                             ).then_inc(s_aln, 1)

            def _xe(qi, k):
                if k == 1:
                    W(a, "a", s_ffnp, 7 * qi + 1)
                    ins = a.copy(out=x1f[:], in_=big[:, 0:E])
                elif k == 2:
                    W(a, "a", s_ffnp, 7 * qi + 2)
                    ins = a.copy(out=qff[:], in_=big[:, 0:E])
                elif k == 3:
                    W(a, "a", s_ffnp, 7 * qi + 3)
                    ins = a.activation(out=esP[:, 0:255], in_=big[:, 0:255],
                                       func=AF.Exp)
                elif k == 4:
                    ins = a.activation(out=esM[:, 0:255], in_=big[:, 0:255],
                                       func=AF.Exp, scale=-1.0)
                elif k == 5:
                    W(a, "a", s_ffnp, 7 * qi + 4)
                    if qi + 1 < NQT and TGs[qi + 1] >= 3:
                        clast = 7 if TGs[qi + 1] >= 8 else 2
                        W(a, "a", s_tc, cum_tc[qi + 1] + clast + 1)
                    ins = a.activation(out=lg[:], in_=big[:, 0:512], func=AF.Ln,
                                       bias=s_onesb[:])
                elif k == 6:
                    W(a, "a", s_ffnp, 7 * qi + 5)
                    ins = a.activation(out=lp[:], in_=big[:, 0:256], func=AF.Exp,
                                       scale=-1.0)
                elif k == 7:
                    W(a, "a", s_ffnp, 7 * qi + 6)
                    ins = a.copy(out=retf[:], in_=big[:, 0:E])
                ins.then_inc(s_ffne, 1)

            for qi in range(NQT + 2):
                qT = qi - 1
                qF = qi - 2
                if qi == 0:
                    a.dma_start(out=qf[0][:], in_=q_sc[0:128, :]
                                ).then_inc(s_qfs[0], 16)
                if qi < NQT:
                    act_head(qi, 0)
                    act_head(qi, 1)
                if 0 <= qT < NQT:
                    act_exp2(qT)
                Tq = [lambda tg=tg: _tre(qT, tg) for tg in range(TGs[qT])] \
                    if 0 <= qT < NQT else []
                xslot = {1: 2, 2: 4, 3: 5, 4: 5, 5: 6, 6: 7, 7: 8}
                ut = 0
                for h in range(H):
                    if qi < NQT and h >= 2:
                        act_head(qi, h)
                    if h == 2 and 0 <= qF < NQT:
                        _rstd(qF, 1)
                    if 0 <= qF < NQT:
                        for k in range(1, 8):
                            if xslot[k] == h:
                                _xe(qF, k)
                    if h >= 9 and ut < len(Tq):
                        Tq[ut]()
                        ut += 1
                    if h == 13 and 0 <= qF < NQT:
                        _rstd(qF, 2)
                    if h == 13 and qi + 1 < NQT:
                        W(a, "a", s_sc, cum_scg[qi])
                        a.dma_start(out=qf[(qi + 1) % 2][:],
                                    in_=q_sc[(qi + 1) * 128:(qi + 2) * 128, :]
                                    ).then_inc(s_qfs[(qi + 1) % 2], 16)
                    if h == 14 and qi < NQT:
                        if qi >= 2:
                            W(a, "a", s_ystr, 16 * (qi - 1))
                        a.dma_start(out=x0[qi % 2][:],
                                    in_=hs[qi * 128:(qi + 1) * 128, :]
                                    ).then_inc(s_x0s[qi % 2], 16)
                while ut < len(Tq):
                    Tq[ut]()
                    ut += 1

    return nc


_NC_CACHE = {}


def _get_nc():
    if "nc" not in _NC_CACHE:
        _NC_CACHE["nc"] = build_nc()
    return _NC_CACHE["nc"]


def _host_constants():
    import jax
    import jax.numpy as jnp

    with jax.default_device(jax.local_devices(backend="cpu")[0]):
        u = jax.random.uniform(jax.random.key(42, impl="threefry2x32"),
                               (8, T, T), jnp.float32, 1e-06, 1.0 - 1e-06)
        g2_all = np.asarray((-jnp.log(-jnp.log(u))) * jnp.float32(0.5),
                            dtype=np.float32)

    smat = np.zeros((512, 256), np.float32)
    for leaf in range(256):
        for lvl in range(8):
            node = (1 << lvl) - 1 + (leaf >> (8 - lvl))
            bit = (leaf >> (7 - lvl)) & 1
            smat[node + 256 * bit, leaf] = 1.0
    idb = np.eye(128, dtype=np.float32).astype(BF)
    idf = np.eye(128, dtype=np.float32)
    col = np.arange(128)
    maskneg = np.where(col[None, :] < col[:, None], 0.0, NEG).astype(np.float32).astype(BF)
    return g2_all, smat, idb, idf, maskneg


def kernel(L0_attn_hidden_states, L0_attn_in_proj_weight, L0_attn_in_proj_bias,
           L0_ffn_Wq, L0_ffn_routers, L0_ffn_leaf_keys, L0_ffn_leaf_values,
           L0_ffn_Wout_w, L0_ffn_Wout_b):
    hs_all = np.ascontiguousarray(np.asarray(L0_attn_hidden_states, np.float32))
    W = np.asarray(L0_attn_in_proj_weight, np.float32)
    bias = np.asarray(L0_attn_in_proj_bias, np.float32)
    wqT = np.ascontiguousarray(W[:E].T)
    wkT = np.ascontiguousarray(W[E:2 * E].T)
    bqt = np.ascontiguousarray(bias[:E].reshape(8, 128).T)
    bkt = np.ascontiguousarray(bias[E:2 * E].reshape(8, 128).T)
    wqfT = np.ascontiguousarray(np.asarray(L0_ffn_Wq, np.float32).T)
    rT = np.ascontiguousarray(np.asarray(L0_ffn_routers, np.float32).T)
    lv = np.ascontiguousarray(np.asarray(L0_ffn_leaf_values, np.float32))
    woT = np.ascontiguousarray(np.asarray(L0_ffn_Wout_w, np.float32).T)
    wob = np.ascontiguousarray(np.asarray(L0_ffn_Wout_b, np.float32).reshape(1, E))

    g2_all, smat, idb, idf, maskneg = _host_constants()

    nc = _get_nc()
    in_maps = []
    for c in range(8):
        in_maps.append(dict(
            hs=np.ascontiguousarray(hs_all[c]), g2=g2_all[c],
            wqT=wqT, wkT=wkT, bqt=bqt, bkt=bkt, wqfT=wqfT, rT=rT, lv=lv,
            woT=woT, wob=wob, smat=smat, idb=idb, idf=idf, maskneg=maskneg,
        ))
    res = run_bass_kernel_spmd(nc, in_maps, core_ids=list(range(8)))
    out = np.stack([np.asarray(res.results[c]["y"], np.float32)
                    for c in range(8)], axis=0)
    return out


# version 29 (final)
# speedup vs baseline: 1.0185x; 1.0185x over previous
import numpy as np
import ml_dtypes
import concourse.bass as bass
import concourse.mybir as mybir
from concourse.bass_utils import run_bass_kernel_spmd
from contextlib import ExitStack

F32 = mybir.dt.float32
BF16 = mybir.dt.bfloat16
AF = mybir.ActivationFunctionType
ALU = mybir.AluOpType
AX = mybir.AxisListType.X
BF = ml_dtypes.bfloat16

T = 4096
E = 1024
H = 16
D = 64
NQT = 32
NEG = -1e9
NHSB = 7

EGs = [qi // 8 + 1 for qi in range(NQT)]
TGs = [qi // 4 + 1 for qi in range(NQT)]
Ks = [128 * (qi + 1) for qi in range(NQT)]

def _cum(xs):
    out = [0]
    for x in xs:
        out.append(out[-1] + x)
    return out

cum_scg = _cum([16 * e for e in EGs])
cum_tg = _cum(TGs)
cum_tc = _cum(TGs)
cum_ch = _cum([qi + 1 for qi in range(NQT)])

e1_of_flat = []
e1_zred_wait = {}
_c1 = 0
for _qi in range(NQT):
    for _h in range(H):
        for _g in range(EGs[_qi]):
            _c1 += 1
            e1_of_flat.append(_c1)
        _c1 += 1
        e1_zred_wait[(_qi, _h)] = _c1
E1_TOTAL = _c1


ln_val = {}
_ct = 0
for _qi in range(NQT + 2):
    if 0 <= _qi - 2 < NQT:
        _ct += 1
        ln_val[("ln2", _qi - 2)] = _ct
    if 0 <= _qi - 1 < NQT:
        _ct += 1
        ln_val[("ln1", _qi - 1)] = _ct

aln_val = {}
for _qi in range(NQT):
    aln_val[("ln1", _qi)] = 2 * _qi + 1
    aln_val[("ln2", _qi)] = 2 * _qi + 2

e2cum = _cum([(tg + 1) // 2 + 1 for tg in TGs])

G_TOTAL = cum_scg[-1]
TG_TOTAL = cum_tg[-1]


def build_nc():
    nc = bass.Bass("TRN2", target_bir_lowering=False)

    hs = nc.dram_tensor("hs", [T, E], F32, kind="ExternalInput").ap()
    g2 = nc.dram_tensor("g2", [T, T], F32, kind="ExternalInput").ap()
    wqT = nc.dram_tensor("wqT", [E, E], F32, kind="ExternalInput").ap()
    wkT = nc.dram_tensor("wkT", [E, E], F32, kind="ExternalInput").ap()
    bqt = nc.dram_tensor("bqt", [128, 8], F32, kind="ExternalInput").ap()
    bkt = nc.dram_tensor("bkt", [128, 8], F32, kind="ExternalInput").ap()
    wqfT = nc.dram_tensor("wqfT", [E, E], F32, kind="ExternalInput").ap()
    rT = nc.dram_tensor("rT", [E, 255], F32, kind="ExternalInput").ap()
    lv = nc.dram_tensor("lv", [256, E], F32, kind="ExternalInput").ap()
    woT = nc.dram_tensor("woT", [E, E], F32, kind="ExternalInput").ap()
    wob = nc.dram_tensor("wob", [1, E], F32, kind="ExternalInput").ap()
    smat = nc.dram_tensor("smat", [512, 256], F32, kind="ExternalInput").ap()
    idb = nc.dram_tensor("idb", [128, 128], BF16, kind="ExternalInput").ap()
    idf = nc.dram_tensor("idf", [128, 128], F32, kind="ExternalInput").ap()
    maskneg = nc.dram_tensor("maskneg", [128, 128], BF16, kind="ExternalInput").ap()

    hs_bf = nc.dram_tensor("hs_bf", [T, E], BF16, kind="Internal").ap()
    q_sc = nc.dram_tensor("q_sc", [T, E], BF16, kind="Internal").ap()

    y = nc.dram_tensor("y", [T, E], F32, kind="ExternalOutput").ap()

    with ExitStack() as ctx:
        sb = lambda n, s, d: ctx.enter_context(nc.sbuf_tensor(n, s, d))
        psa = lambda n, s, d: ctx.enter_context(nc.psum_tensor(n, s, d))
        sem = lambda n: ctx.enter_context(nc.semaphore(n))

        k_fm = sb("k_fm", [128, 8 * T], BF16)
        arena = sb("arena", [128, 20480], BF16)
        w_s = sb("w_s", [128, 4 * 256], F32)
        w_wob_bf = sb("w_wob_bf", [1, E], BF16)
        s_idb = sb("s_idb", [128, 128], BF16)
        s_idf = sb("s_idf", [128, 128], F32)
        s_mask = sb("s_mask", [128, 128], BF16)
        s_bqt = sb("s_bqt", [128, 8], F32)
        s_bkt = sb("s_bkt", [128, 8], F32)
        s_ones1 = sb("s_ones1", [1, 128], BF16)
        s_onesb = sb("s_onesb", [128, 1], F32)
        s_eps = sb("s_eps", [128, 1], F32)
        jnka = sb("jnka", [128, 1], F32)
        jnkv = sb("jnkv", [128, 1], F32)
        jnkp = sb("jnkp", [128, 1], F32)

        E0 = sb("E0", [128, T], BF16)
        E1 = sb("E1", [128, T], BF16)
        Ebufs = [E0, E1]
        avg = sb("avg", [128, T], F32)
        dw = sb("dw", [128, T], BF16)
        g2q = [sb(f"g2q{i}", [128, 512], F32) for i in range(2)]
        qf = [sb(f"qf{i}", [128, E], BF16) for i in range(2)]
        x0 = [sb(f"x0{i}", [128, E], F32) for i in range(2)]
        hsb = [sb(f"hsb{i}", [128, E], BF16) for i in range(NHSB)]
        dwT = [sb(f"dwT{i}", [128, 512], BF16) for i in range(2)]
        Zp = sb("Zp", [128, 64], F32)
        Zt = sb("Zt", [128, 2], F32)
        c_h = sb("c_h", [128, 2], F32)
        Z2 = sb("Z2", [128, 1], F32)
        Z2p = sb("Z2p", [128, 8], F32)
        rZ2 = sb("rZ2", [128, 1], F32)
        s1 = sb("s1", [128, E], F32)
        x1 = sb("x1", [128, E], F32)
        msum = sb("msum", [128, 1], F32)
        m2 = sb("m2", [128, 1], F32)
        negm2 = sb("negm2", [128, 1], F32)
        msq = sb("msq", [128, 1], F32)
        ssq = sb("ssq", [128, 1], F32)
        lnv = sb("lnv", [128, 1], F32)
        rstd = sb("rstd", [128, 1], F32)
        x1f = sb("x1f", [128, E], BF16)
        qff = x1f
        esP = sb("esP", [128, 256], F32)
        esM = sb("esM", [128, 256], F32)
        lg = sb("lg", [128, 512], F32)
        lp = sb("lp", [128, 256], BF16)
        retf = sb("retf", [128, E], BF16)
        yt = None
        w_wob = avg[0:1, 3072:4096]
        wstage = [sb("wstage0", [128, 1024], F32), avg[:, 2048:3072]]
        pro_hs = [avg[:, 0:1024], avg[:, 1024:2048]]
        pro_hsb = [arena[:, 16384:17408], arena[:, 17408:18432]]
        hfm = [arena[:, 18432:19456], arena[:, 19456:20480]]
        q_pro = [E0[:, 0:1024], E0[:, 1024:2048]]

        scps = [psa("p_scA", [128, 1024], F32), psa("p_scB", [128, 1024], F32)]
        big = psa("p_big", [128, 1024], F32)
        trps = [psa("p_trA", [128, 1024], BF16), psa("p_trB", [128, 1024], BF16)]
        PREG = [big, scps[0], scps[1]]

        s_wconst = sem("s_wconst")
        s_wst = sem("s_wst")
        s_ph = sem("s_ph")
        s_hsw = sem("s_hsw")
        s_qsw = sem("s_qsw")
        s_ystr = sem("s_ystr")
        s_qfs = [sem("s_qf0"), sem("s_qf1")]
        s_x0s = [sem("s_x00"), sem("s_x01")]
        s_g2s = [sem(f"s_g2{i}") for i in range(7)]
        s_mev = sem("s_mev")
        s_hsbs = [sem(f"s_hsb{i}") for i in range(NHSB)]
        s_dvec = sem("s_dvec")
        s_poolc = sem("s_poolc")
        s_ppro = sem("s_ppro")
        s_apro = sem("s_apro")
        s_sc = sem("s_sc")
        s_e1 = sem("s_e1")
        s_stt = sem("s_stt")
        s_tc = sem("s_tc")
        s_e2 = sem("s_e2")
        s_ptr2 = sem("s_ptr2")
        s_atrev = sem("s_atrev")
        s_mix = sem("s_mix")
        s_s1 = sem("s_s1")
        s_ln = sem("s_ln")
        s_aln = sem("s_aln")
        s_x1 = sem("s_x1")
        s_ffnp = sem("s_ffnp")
        s_ffne = sem("s_ffne")
        s_y = sem("s_y")
        s_zr = sem("s_zr")
        s_rc = sem("s_rc")
        s_rz = sem("s_rz")
        s_ar = sem("s_ar")
        s_fw = [sem("s_fw0"), sem("s_fw1")]
        s_wbf = sem("s_wbf")

        def ffn_slot(i):
            if i < 8:
                return i * 1024, 1024
            if i < 16:
                return 8192 + (i - 8) * 256, 255
            if i < 18:
                return 10240 + (i - 16) * 1024, 1024
            return 12288 + (i - 18) * 1024, 1024

        def ffn_src(i):
            if i < 8:
                return wqfT, i * 128
            if i < 16:
                return rT, (i - 8) * 128
            if i < 18:
                return lv, (i - 16) * 128
            return woT, (i - 18) * 128

        chunk_consumer = {}
        for qi in range(NQT):
            for j in range(qi + 1):
                chunk_consumer[cum_ch[qi] + j] = cum_tg[qi] + j // 4

        g2_load_war = {}
        g2_cnt = {}
        _g2_last = {}
        _g2_n = [0] * 7
        for qi in range(NQT):
            for c in range(TGs[qi]):
                b = c % 7
                waits = []
                if c >= 7:
                    waits.append(("tc", cum_tc[qi] + (c - 7) + 1))
                elif b in (5, 6):
                    waits.append(("mev", qi))
                else:
                    if b == 2 and qi >= 2:
                        waits.append(("ffnp", 7 * (qi - 2) + 5))
                    if b in (3, 4) and qi >= 1:
                        waits.append(("s1", qi - 1))
                    if b in _g2_last:
                        pq, pc = _g2_last[b]
                        waits.append(("tc", cum_tc[pq] + pc + 1))
                _g2_last[b] = (qi, c)
                _g2_n[b] += 1
                g2_cnt[(qi, c)] = _g2_n[b]
                g2_load_war[(qi, c)] = waits

        def g2buf(qi, c):
            b = c % 7
            if b < 2:
                return g2q[b]
            if b == 2:
                return lg[:, 0:512]
            if b in (3, 4):
                return x1[:, (b - 3) * 512:(b - 2) * 512]
            return x0[(qi + 1) % 2][:, (b - 5) * 512:(b - 4) * 512]

        last_wait = {}

        def W(eng, key, semh, val):
            if val <= 0:
                return
            k = (key, id(semh))
            if last_wait.get(k, -1) >= val:
                return
            last_wait[k] = val
            eng.wait_ge(semh, val)

        block = ctx.enter_context(nc.Block())

        @block.gpsimd
        def _(g):
            g.dma_start(out=s_idb[:], in_=idb).then_inc(s_wconst, 16)
            g.dma_start(out=s_idf[:], in_=idf).then_inc(s_wconst, 16)
            g.dma_start(out=s_mask[:], in_=maskneg).then_inc(s_wconst, 16)
            g.dma_start(out=s_bqt[:], in_=bqt).then_inc(s_wconst, 16)
            g.dma_start(out=s_bkt[:], in_=bkt).then_inc(s_wconst, 16)
            g.dma_start(out=w_wob[:], in_=wob).then_inc(s_wconst, 16)
            for i in range(4):
                g.dma_start(out=w_s[:, i * 256:(i + 1) * 256],
                            in_=smat[i * 128:(i + 1) * 128, :]).then_inc(s_wconst, 16)

            def _wcast(i):
                kc = i % 8
                base = kc * 1024 if i < 8 else 8192 + kc * 1024
                W(g, "g", s_wst, 16 * (i + 1))
                g.tensor_copy(out=arena[:, base:base + 1024],
                              in_=wstage[0][:]).then_inc(s_poolc, 1)

            for i in range(16):
                src_t = wqT if i < 8 else wkT
                kc = i % 8
                g.dma_start(out=wstage[0][:],
                            in_=src_t[kc * 128:(kc + 1) * 128, :]).then_inc(s_wst, 16)
                _wcast(i)
            for tc in range(2):
                W(g, "g", s_ph, 16 * tc)
                g.dma_start(out=pro_hs[tc % 2][:],
                            in_=hs[tc * 128:(tc + 1) * 128, :]).then_inc(s_ph, 16)
            for tc in range(NQT):
                W(g, "g", s_dvec, tc + 1)
                W(g, "g", s_hsw, 16 * tc)
                g.dma_start(out=hs_bf[tc * 128:(tc + 1) * 128, :],
                            in_=pro_hsb[tc % 2][:]).then_inc(s_hsw, 16)
                W(g, "g", s_apro, 18 * tc + 10)
                W(g, "g", s_qsw, 16 * tc)
                g.dma_start(out=q_sc[tc * 128:(tc + 1) * 128, :],
                            in_=q_pro[tc % 2][:]).then_inc(s_qsw, 16)
                if tc + 2 < NQT:
                    W(g, "g", s_ph, 16 * (tc + 2))
                    g.dma_start(out=pro_hs[tc % 2][:],
                                in_=hs[(tc + 2) * 128:(tc + 3) * 128, :]).then_inc(s_ph, 16)
            for qi in range(NQT):
                W(g, "g", s_y, qi + 1)
                W(g, "g", s_ystr, 16 * qi)
                g.dma_start(out=y[qi * 128:(qi + 1) * 128, :],
                            in_=x0[qi % 2][:]).then_inc(s_ystr, 16)
            g.wait_ge(s_ystr, 16 * NQT)
            g.wait_ge(s_hsw, 16 * NQT)
            g.wait_ge(s_qsw, 16 * NQT)
            g.wait_ge(s_wst, 16 * 16)
            g.wait_ge(s_wconst, 16 * 10)
            g.wait_ge(s_ph, 16 * NQT)

        @block.sync
        def _(s):
            for qi in range(NQT):
                if qi == 1:
                    W(s, "s", s_ppro, 3 * NQT)
                    for i in range(26):
                        src_t, r0 = ffn_src(i)
                        ncols = 255 if (8 <= i < 16) else 1024
                        if i >= 2:
                            W(s, "s", s_poolc, 16 + (i - 1))
                        s.dma_start(out=wstage[i % 2][:, 0:ncols],
                                    in_=src_t[r0:r0 + 128, 0:ncols]
                                    ).then_inc(s_fw[i % 2], 16)
                for c in range(TGs[qi]):
                    for kind, val in g2_load_war[(qi, c)]:
                        W(s, "s", {"tc": s_tc, "mev": s_mev, "s1": s_s1,
                                   "ffnp": s_ffnp}[kind], val)
                    c0 = c * 512
                    c1 = min(Ks[qi], c0 + 512)
                    s.dma_start(out=g2buf(qi, c)[:, 0:c1 - c0],
                                in_=g2[qi * 128:(qi + 1) * 128, c0:c1]
                                ).then_inc(s_g2s[c % 7], 16)
                for j in range(qi + 1):
                    gm = cum_ch[qi] + j
                    W(s, "s", s_hsw, 16 * (j + 1))
                    if gm >= NHSB:
                        W(s, "s", s_mix, chunk_consumer[gm - NHSB] + 1)
                    s.dma_start(out=hsb[gm % NHSB][:],
                                in_=hs_bf[j * 128:(j + 1) * 128, :]
                                ).then_inc(s_hsbs[gm % NHSB], 16)

        @block.vector
        def _(v):
            v.memset(s_ones1[:], 1.0)
            v.memset(s_onesb[:], 1.0)
            v.memset(s_eps[:], 1e-5)
            v.memset(esP[:, 255:256], 0.0)
            v.memset(esM[:, 255:256], 0.0)
            W(v, "v", s_wconst, 16 * 10)
            v.tensor_copy(out=w_wob_bf[:], in_=w_wob[:]).then_inc(s_wbf, 1)
            for tc in range(NQT):
                W(v, "v", s_ph, 16 * (tc + 1))
                if tc >= 2:
                    W(v, "v", s_ppro, 3 * (tc - 2) + 1)
                    W(v, "v", s_hsw, 16 * (tc - 1))
                v.tensor_copy(out=pro_hsb[tc % 2][:], in_=pro_hs[tc % 2][:]
                              ).then_inc(s_dvec, 1)
            zr_cnt = [0]
            rz_cnt = [0]
            fc_idx = [0]

            def dve_fc(n=2):
                for _ in range(n):
                    i = fc_idx[0]
                    if i >= 26:
                        return
                    fc_idx[0] += 1
                    if i == 18:
                        W(v, "v", s_hsw, 16 * NQT)
                    base, ncols = ffn_slot(i)
                    W(v, "v", s_fw[i % 2], 16 * (i // 2 + 1))
                    v.tensor_copy(out=arena[:, base:base + ncols],
                                  in_=wstage[i % 2][:, 0:ncols]
                                  ).then_inc(s_poolc, 1)

            def st_recip(qi, h):
                if EGs[qi] == 1:
                    srcz = Zp[:, h * 4:h * 4 + 1]
                else:
                    srcz = Zt[:, h % 2:h % 2 + 1]
                v.reciprocal(out=c_h[:, h % 2:h % 2 + 1], in_=srcz
                             ).then_inc(s_rc, 1)

            def st_stt(qi, h):
                K = Ks[qi]
                cs = c_h[:, h % 2:h % 2 + 1]
                W(v, "v", s_rc, 16 * qi + h + 1)
                if h == 0:
                    W(v, "v", s_e2, e2cum[qi])
                    v.tensor_scalar(out=avg[:, 0:K], in0=Ebufs[h % 2][:, 0:K],
                                    scalar1=cs, scalar2=None,
                                    op0=ALU.mult).then_inc(s_stt, 1)
                else:
                    v.scalar_tensor_tensor(out=avg[:, 0:K],
                                           in0=Ebufs[h % 2][:, 0:K],
                                           scalar=cs, in1=avg[:, 0:K],
                                           op0=ALU.mult, op1=ALU.add
                                           ).then_inc(s_stt, 1)

            def dve_head_seq(qi, h):
                W(v, "v", s_e1, e1_zred_wait[(qi, h)])
                if qi == 0:
                    v.memset(Zp[0:1, h * 4:h * 4 + 1], 1.0)
                    v.drain()
                st_recip(qi, h)
                v.drain()
                st_stt(qi, h)

            def dve_tcomb(qi, crange):
                K = Ks[qi]
                for c in crange:
                    c0 = c * 512
                    c1 = min(K, c0 + 512)
                    W(v, "v", s_g2s[c % 7], 16 * g2_cnt[(qi, c)])
                    v.scalar_tensor_tensor(out=avg[:, c0:c1], in0=avg[:, c0:c1],
                                           scalar=1.0 / 32.0,
                                           in1=g2buf(qi, c)[:, 0:c1 - c0],
                                           op0=ALU.mult, op1=ALU.add).then_inc(s_tc, 1)

            def dve_z2chain(qi):
                W(v, "v", s_e2, e2cum[qi + 1])
                rz_cnt[0] += 1
                v.tensor_reduce(out=Z2[:], in_=Z2p[:, 0:(TGs[qi] + 1) // 2],
                                axis=AX, op=ALU.add).then_inc(s_rz, 1)
                if qi == 0:
                    v.drain()
                    v.memset(Z2[0:1, 0:1], 1.0)
                    v.drain()
                W(v, "v", s_rz, rz_cnt[0])
                rz_cnt[0] += 1
                v.reciprocal(out=rZ2[:], in_=Z2[:]).then_inc(s_rz, 1)

            def dve_mix(qi):
                W(v, "v", s_mix, cum_tg[qi + 1])
                W(v, "v", s_x0s[qi % 2], 16 * (qi // 2 + 1))
                W(v, "v", s_rz, 2 * (qi + 1))
                v.scalar_tensor_tensor(out=s1[:], in0=big[:, 0:E], scalar=rZ2[:],
                                       in1=x0[qi % 2][:], op0=ALU.mult,
                                       op1=ALU.add, accum_out=msum[:]
                                       ).then_inc(s_mev, 1)

            def _ln_core(qi, which, sqdst, outdst, outsem):
                v.scalar_tensor_tensor(out=sqdst[:], in0=s1[:], scalar=1.0,
                                       in1=s1[:], op0=ALU.mult, op1=ALU.mult,
                                       accum_out=ssq[:])
                v.scalar_tensor_tensor(out=negm2[:], in0=msum[:],
                                       scalar=-1.0 / (1024.0 * 1024.0),
                                       in1=msum[:], op0=ALU.mult, op1=ALU.mult)
                v.tensor_scalar(out=m2[:], in0=msum[:], scalar1=1.0 / 1024.0,
                                scalar2=None, op0=ALU.mult)
                v.tensor_scalar(out=msq[:], in0=negm2[:], scalar1=1e-5,
                                scalar2=None, op0=ALU.add).then_inc(s_ln, 1)
                W(v, "v", s_aln, aln_val[(which, qi)])
                v.tensor_scalar(out=outdst[:], in0=s1[:], scalar1=m2[:],
                                scalar2=rstd[:], op0=ALU.subtract,
                                op1=ALU.mult).then_inc(outsem, 1)

            def dve_ln1a(qi):
                if qi >= 1:
                    W(v, "v", s_ffnp, 7 * (qi - 1) + 3)
                v.scalar_tensor_tensor(out=qff[:], in0=s1[:], scalar=1.0,
                                       in1=s1[:], op0=ALU.mult, op1=ALU.mult,
                                       accum_out=ssq[:])
                v.scalar_tensor_tensor(out=negm2[:], in0=msum[:],
                                       scalar=-1.0 / (1024.0 * 1024.0),
                                       in1=msum[:], op0=ALU.mult, op1=ALU.mult)
                v.tensor_scalar(out=m2[:], in0=msum[:], scalar1=1.0 / 1024.0,
                                scalar2=None, op0=ALU.mult)
                v.tensor_scalar(out=msq[:], in0=negm2[:], scalar1=1e-5,
                                scalar2=None, op0=ALU.add).then_inc(s_ln, 1)

            def dve_ln1b(qi):
                W(v, "v", s_aln, aln_val[("ln1", qi)])
                v.tensor_scalar(out=x1[:], in0=s1[:], scalar1=m2[:],
                                scalar2=rstd[:], op0=ALU.subtract,
                                op1=ALU.mult).then_inc(s_x1, 1)

            def dve_res2(qi):
                W(v, "v", s_ffnp, 7 * qi + 7)
                v.scalar_tensor_tensor(out=s1[:], in0=big[:, 0:E], scalar=1.0,
                                       in1=x1[:], op0=ALU.mult, op1=ALU.add,
                                       accum_out=msum[:]).then_inc(s_s1, 1)

            def dve_ln2(qi):
                _ln_core(qi, "ln2", x1f, x0[qi % 2], s_y)

            for qi in range(NQT + 2):
                if qi == 0:
                    for h in range(H):
                        dve_head_seq(qi, h)
                        dve_fc()
                elif qi < NQT:
                    EG = EGs[qi]
                    for h in range(H):
                        if h >= 1:
                            st_stt(qi, h - 1)
                        if h == 1 and 0 <= qi - 2 < NQT:
                            dve_ln1a(qi - 2)
                        if h == 2 and 0 <= qi - 2 < NQT:
                            dve_ln1b(qi - 2)
                        if h == 10 and 0 <= qi - 2 < NQT:
                            dve_res2(qi - 2)
                        if h == 13 and 0 <= qi - 2 < NQT:
                            dve_ln2(qi - 2)
                        if h == 14 and 0 <= qi - 1 < NQT:
                            dve_z2chain(qi - 1)
                        W(v, "v", s_e1, e1_zred_wait[(qi, h)])
                        if EG > 1:
                            zr_cnt[0] += 1
                            v.tensor_reduce(out=Zt[:, h % 2:h % 2 + 1],
                                            in_=Zp[:, h * 4:h * 4 + EG],
                                            axis=AX, op=ALU.add
                                            ).then_inc(s_zr, 1)
                            W(v, "v", s_zr, zr_cnt[0])
                        st_recip(qi, h)
                        if qi <= 3:
                            dve_fc()
                    st_stt(qi, H - 1)
                else:
                    if 0 <= qi - 2 < NQT:
                        dve_ln1a(qi - 2)
                        dve_ln1b(qi - 2)
                        dve_res2(qi - 2)
                        dve_ln2(qi - 2)
                    if 0 <= qi - 1 < NQT:
                        dve_z2chain(qi - 1)
                if qi < NQT:
                    dve_tcomb(qi, list(range(min(2, TGs[qi]))))
                if 0 <= qi - 1 < NQT:
                    dve_mix(qi - 1)
                if qi < NQT:
                    dve_tcomb(qi, list(range(2, TGs[qi])))

        @block.tensor
        def _(t):
            W(t, "t", s_wconst, 16 * 10)
            W(t, "t", s_poolc, 16)
            for tc in range(NQT):
                W(t, "t", s_dvec, tc + 1)
                W(t, "t", s_apro, 18 * (tc - 1) + 2)
                for j in range(8):
                    tp = trps[j // 4]
                    ins = t.transpose(tp[:, (j % 4) * 128:(j % 4) * 128 + 128],
                                      pro_hsb[tc % 2][:, j * 128:(j + 1) * 128],
                                      s_idb[:])
                    if j == 7:
                        ins.then_inc(s_ppro, 1)
                W(t, "t", s_apro, 18 * tc + 2)
                uq = 2 * tc
                if uq >= 3:
                    up = uq - 3
                    W(t, "t", s_apro,
                      18 * (up // 2) + (10 if up % 2 == 0 else 18))
                for eo in range(8):
                    for kc in range(8):
                        ins = t.matmul(PREG[uq % 3][:, eo * 128:(eo + 1) * 128],
                                       arena[:, kc * 1024 + eo * 128:kc * 1024 + eo * 128 + 128],
                                       hfm[tc % 2][:, kc * 128:(kc + 1) * 128],
                                       start=(kc == 0), stop=(kc == 7))
                        if eo == 7 and kc == 7:
                            ins.then_inc(s_ppro, 1)
                uk = 2 * tc + 1
                if uk >= 3:
                    up = uk - 3
                    W(t, "t", s_apro,
                      18 * (up // 2) + (10 if up % 2 == 0 else 18))
                for eo in range(8):
                    for kc in range(8):
                        ins = t.matmul(
                            PREG[uk % 3][:, eo * 128:(eo + 1) * 128],
                            arena[:, 8192 + kc * 1024 + eo * 128:8192 + kc * 1024 + eo * 128 + 128],
                            hfm[tc % 2][:, kc * 128:(kc + 1) * 128],
                            start=(kc == 0), stop=(kc == 7))
                        if eo == 7 and kc == 7:
                            ins.then_inc(s_ppro, 1)
            def pe_sc_head(qi, h, pieces=()):
                K = Ks[qi]
                EG = EGs[qi]
                pi = 0
                for gloc in range(EG):
                    g = cum_scg[qi] + h * EG + gloc
                    if qi == 0 and h == 0 and gloc == 0:
                        W(t, "t", s_apro, 18 * NQT)
                    if gloc == 0:
                        W(t, "t", s_qfs[qi % 2], 16 * (qi // 2 + 1))
                    if g >= 2:
                        W(t, "t", s_e1, e1_of_flat[g - 2])
                    ps = scps[g % 2]
                    c0 = gloc * 1024
                    c1 = min(K, c0 + 1024)
                    p0 = (h & 1) * 64
                    lhs = qf[qi % 2][p0:p0 + 64, (h >> 1) * 128:(h >> 1) * 128 + 128]
                    base = (h >> 1) * T
                    nmm = ((c1 - c0) + 511) // 512
                    isdiag = (gloc == EG - 1)
                    for mmi in range(nmm):
                        a = c0 + mmi * 512
                        b = min(c1, a + 512)
                        last = (mmi == nmm - 1)
                        ins = t.matmul(ps[:, a - c0:b - c0], lhs,
                                       k_fm[p0:p0 + 64, base + a:base + b],
                                       start=True, stop=not (isdiag and last))
                        if last and not isdiag:
                            ins.then_inc(s_sc, 1)
                    if isdiag:
                        t.matmul(ps[:, K - 128 - c0:K - c0], s_idb[:],
                                 s_mask[:], start=False, stop=True
                                 ).then_inc(s_sc, 1)
                    if pi < len(pieces):
                        pieces[pi]()
                        pi += 1
                while pi < len(pieces):
                    pieces[pi]()
                    pi += 1

            def _trhalf(qi, tg, half):
                gtg = cum_tg[qi] + tg
                nch = min(4, (qi + 1) - tg * 4)
                i0 = 0 if half == 0 else (nch + 1) // 2
                i1 = (nch + 1) // 2 if half == 0 else nch
                if i0 >= i1:
                    return
                W(t, "t", s_e2, e2cum[qi] + tg // 2 + 1)
                W(t, "t", s_atrev, gtg - 1)
                for ii in range(i0, i1):
                    j = tg * 4 + ii
                    ins = t.transpose(trps[gtg % 2][:, ii * 128:(ii + 1) * 128],
                                      dw[:, j * 128:(j + 1) * 128], s_idb[:])
                    if ii == nch - 1:
                        ins.then_inc(s_ptr2, 1)

            def _mixchunk(qi, tg, ii):
                gtg = cum_tg[qi] + tg
                nch = min(4, (qi + 1) - tg * 4)
                W(t, "t", s_atrev, gtg + 1)
                if tg == 0 and ii == 0:
                    W(t, "t", s_s1, qi)
                j = tg * 4 + ii
                gm = cum_ch[qi] + j
                W(t, "t", s_hsbs[gm % NHSB], 16 * (gm // NHSB + 1))
                for half in range(2):
                    ins = t.matmul(big[:, half * 512:(half + 1) * 512],
                                   dwT[gtg % 2][:, ii * 128:(ii + 1) * 128],
                                   hsb[gm % NHSB][:, half * 512:(half + 1) * 512],
                                   start=(j == 0), stop=(j == qi))
                    if ii == nch - 1 and half == 1:
                        ins.then_inc(s_mix, 1)

            def _ffn1(qi, half):
                W(t, "t", s_x1, qi + 1)
                for j in range(half * 4, half * 4 + 4):
                    ins = t.transpose(big[:, j * 128:(j + 1) * 128],
                                      x1[:, j * 128:(j + 1) * 128], s_idf[:])
                    if j == 7:
                        ins.then_inc(s_ffnp, 1)

            def _ffn2(qi, quarter):
                W(t, "t", s_ffne, 7 * qi + 1)
                if qi == 0:
                    W(t, "t", s_poolc, 16 + 8)
                for eo in range(quarter * 2, quarter * 2 + 2):
                    for kc in range(8):
                        ins = t.matmul(big[:, eo * 128:(eo + 1) * 128],
                                       arena[:, kc * 1024 + eo * 128:kc * 1024 + eo * 128 + 128],
                                       x1f[:, kc * 128:(kc + 1) * 128],
                                       start=(kc == 0), stop=(kc == 7))
                        if eo == 7 and kc == 7:
                            ins.then_inc(s_ffnp, 1)

            def _ffn3(qi, half):
                W(t, "t", s_ffne, 7 * qi + 2)
                if qi == 0:
                    W(t, "t", s_poolc, 16 + 16)
                for kc in range(half * 4, half * 4 + 4):
                    ins = t.matmul(big[:, 0:255],
                                   qff[:, kc * 128:(kc + 1) * 128],
                                   arena[:, 8192 + kc * 256:8192 + kc * 256 + 255],
                                   start=(kc == 0), stop=(kc == 7))
                    if kc == 7:
                        ins.then_inc(s_ffnp, 1)

            def _ffn4(qi):
                W(t, "t", s_ffne, 7 * qi + 4)
                for j in range(4):
                    srct = esP if j < 2 else esM
                    ins = t.transpose(big[:, j * 128:(j + 1) * 128],
                                      srct[:, (j % 2) * 128:(j % 2) * 128 + 128],
                                      s_idf[:])
                    if j == 3:
                        ins.then_inc(s_ffnp, 1)

            def _ffn5(qi):
                W(t, "t", s_ffne, 7 * qi + 5)
                for lc in range(2):
                    for kc in range(4):
                        ins = t.matmul(big[:, lc * 128:(lc + 1) * 128],
                                       w_s[:, kc * 256 + lc * 128:kc * 256 + lc * 128 + 128],
                                       lg[:, kc * 128:(kc + 1) * 128],
                                       start=(kc == 0), stop=(kc == 3))
                        if lc == 1 and kc == 3:
                            ins.then_inc(s_ffnp, 1)

            def _ffn6(qi, half):
                W(t, "t", s_ffne, 7 * qi + 6)
                if qi == 0:
                    W(t, "t", s_poolc, 16 + 18)
                for dj in range(half * 4, half * 4 + 4):
                    for lc in range(2):
                        ins = t.matmul(big[:, dj * 128:(dj + 1) * 128],
                                       arena[:, 10240 + lc * 1024 + dj * 128:10240 + lc * 1024 + dj * 128 + 128],
                                       lp[:, lc * 128:(lc + 1) * 128],
                                       start=(lc == 0), stop=(lc == 1))
                        if dj == 7 and lc == 1:
                            ins.then_inc(s_ffnp, 1)

            def _ffn7(qi, half):
                W(t, "t", s_ffne, 7 * qi + 7)
                if qi == 0:
                    W(t, "t", s_poolc, 16 + 26)
                    W(t, "t", s_wbf, 1)
                for kc in range(8):
                    t.matmul(big[:, half * 512:(half + 1) * 512],
                             retf[:, kc * 128:(kc + 1) * 128],
                             arena[:, 12288 + kc * 1024 + half * 512:12288 + kc * 1024 + half * 512 + 512],
                             start=(kc == 0), stop=False)
                ins = t.matmul(big[:, half * 512:(half + 1) * 512],
                               s_ones1[:], w_wob_bf[:, half * 512:(half + 1) * 512],
                               start=False, stop=True)
                if half == 1:
                    ins.then_inc(s_ffnp, 1)

            for qi in range(NQT + 2):
                qT = qi - 1
                qF = qi - 2
                TM = []
                if 0 <= qT < NQT:
                    def _trps(tg):
                        return [lambda hh=hh, tg=tg: _trhalf(qT, tg, hh)
                                for hh in range(2)]

                    def _mxps(tg):
                        nch = min(4, (qT + 1) - tg * 4)
                        return [lambda ii=ii, tg=tg: _mixchunk(qT, tg, ii)
                                for ii in range(nch)]
                    for tg in range(TGs[qT]):
                        TM.append(_trps(tg))
                        if tg >= 1:
                            TM.append(_mxps(tg - 1))
                    TM.append(_mxps(TGs[qT] - 1))
                if 0 <= qF < NQT:
                    Fq = [
                        [lambda hh=hh: _ffn1(qF, hh) for hh in range(2)],
                        [lambda qq=qq: _ffn2(qF, qq) for qq in range(4)],
                        [lambda hh=hh: _ffn3(qF, hh) for hh in range(2)],
                        [lambda: _ffn4(qF)],
                        [lambda: _ffn5(qF)],
                        [lambda hh=hh: _ffn6(qF, hh) for hh in range(2)],
                        [lambda hh=hh: _ffn7(qF, hh) for hh in range(2)],
                    ]
                else:
                    Fq = []
                ut = uf = 0
                pending = []
                for h in range(H):
                    give, pending = pending[:2], pending[2:]
                    if qi < NQT:
                        pe_sc_head(qi, h, give)
                    else:
                        for p in give:
                            p()
                    if 2 <= h <= 8 and uf < len(Fq):
                        pending += Fq[uf]
                        uf += 1
                    if h >= 9:
                        for _ in range(2):
                            if ut < len(TM):
                                for p in TM[ut]:
                                    p()
                                ut += 1
                for p in pending:
                    p()
                while uf < len(Fq):
                    for p in Fq[uf]:
                        p()
                    uf += 1
                while ut < len(TM):
                    for p in TM[ut]:
                        p()
                    ut += 1

        @block.scalar
        def _(a):
            for tc in range(NQT):
                W(a, "a", s_ppro, 3 * tc + 1)
                a.copy(out=hfm[tc % 2][:, 0:512], in_=trps[0][:, 0:512]
                       ).then_inc(s_apro, 1)
                a.copy(out=hfm[tc % 2][:, 512:1024], in_=trps[1][:, 0:512]
                       ).then_inc(s_apro, 1)
                W(a, "a", s_ppro, 3 * tc + 2)
                if tc >= 2:
                    W(a, "a", s_qsw, 16 * (tc - 1))
                for eo in range(8):
                    ins = a.activation(out=q_pro[tc % 2][:, eo * 128:(eo + 1) * 128],
                                       in_=PREG[(2 * tc) % 3][:, eo * 128:(eo + 1) * 128],
                                       func=AF.Identity,
                                       bias=s_bqt[:, eo:eo + 1])
                    ins.then_inc(s_apro, 1)
                W(a, "a", s_ppro, 3 * tc + 3)
                for eo in range(8):
                    ins = a.activation(out=k_fm[:, eo * T + tc * 128:eo * T + tc * 128 + 128],
                                       in_=PREG[(2 * tc + 1) % 3][:, eo * 128:(eo + 1) * 128],
                                       func=AF.Identity,
                                       bias=s_bkt[:, eo:eo + 1])
                    ins.then_inc(s_apro, 1)
            W(a, "a", s_qsw, 16 * NQT)

            def act_exp2(qi):
                K = Ks[qi]
                W(a, "a", s_ptr2, cum_tg[qi])
                W(a, "a", s_rz, 2 * qi - 1)
                nch = (TGs[qi] + 1) // 2
                for c in range(nch):
                    c0 = c * 1024
                    c1 = min(K, c0 + 1024)
                    last = (c == nch - 1)
                    W(a, "a", s_tc,
                      cum_tc[qi + 1] if last else cum_tc[qi] + 2 * c + 2)
                    a.activation(out=dw[:, c0:c1], in_=avg[:, c0:c1],
                                 func=AF.Exp, accum_out=Z2p[:, c:c + 1]
                                 ).then_inc(s_e2, 1)
                a.copy(out=jnka[:], in_=s_eps[:]).then_inc(s_e2, 1)

            def act_head(qi, h):
                K = Ks[qi]
                EG = EGs[qi]
                nh = 16 * qi + h
                W(a, "a", s_stt, nh - 1)
                for gloc in range(EG):
                    g = cum_scg[qi] + h * EG + gloc
                    c0 = gloc * 1024
                    c1 = min(K, c0 + 1024)
                    W(a, "a", s_sc, g + 1)
                    ins = a.activation(out=Ebufs[h % 2][:, c0:c1],
                                       in_=scps[g % 2][:, 0:c1 - c0],
                                       func=AF.Exp, scale=0.125,
                                       accum_out=Zp[:, h * 4 + gloc:h * 4 + gloc + 1])
                    ins.then_inc(s_e1, 1)
                a.copy(out=jnka[:], in_=s_eps[:]).then_inc(s_e1, 1)

            def _tre(qi, tg):
                gtg = cum_tg[qi] + tg
                nch = min(4, (qi + 1) - tg * 4)
                W(a, "a", s_ptr2, gtg + 1)
                W(a, "a", s_mix, gtg - 1)
                a.copy(out=dwT[gtg % 2][:, 0:nch * 128],
                       in_=trps[gtg % 2][:, 0:nch * 128]).then_inc(s_atrev, 1)

            rstd_cnt = [0]

            def _rstd(qi, which):
                W(a, "a", s_ln, ln_val[("ln1" if which == 1 else "ln2", qi)])
                a.activation(out=lnv[:], in_=ssq[:], func=AF.Ln,
                             bias=msq[:], scale=1.0 / 1024.0).then_inc(s_ar, 1)
                rstd_cnt[0] += 1
                W(a, "a", s_ar, rstd_cnt[0])
                a.activation(out=rstd[:], in_=lnv[:], func=AF.Exp, scale=-0.5
                             ).then_inc(s_aln, 1)

            def _xe(qi, k):
                if k == 1:
                    W(a, "a", s_ffnp, 7 * qi + 1)
                    ins = a.copy(out=x1f[:], in_=big[:, 0:E])
                elif k == 2:
                    W(a, "a", s_ffnp, 7 * qi + 2)
                    ins = a.copy(out=qff[:], in_=big[:, 0:E])
                elif k == 3:
                    W(a, "a", s_ffnp, 7 * qi + 3)
                    ins = a.activation(out=esP[:, 0:255], in_=big[:, 0:255],
                                       func=AF.Exp)
                elif k == 4:
                    ins = a.activation(out=esM[:, 0:255], in_=big[:, 0:255],
                                       func=AF.Exp, scale=-1.0)
                elif k == 5:
                    W(a, "a", s_ffnp, 7 * qi + 4)
                    if qi + 1 < NQT and TGs[qi + 1] >= 3:
                        clast = 7 if TGs[qi + 1] >= 8 else 2
                        W(a, "a", s_tc, cum_tc[qi + 1] + clast + 1)
                    ins = a.activation(out=lg[:], in_=big[:, 0:512], func=AF.Ln,
                                       bias=s_onesb[:])
                elif k == 6:
                    W(a, "a", s_ffnp, 7 * qi + 5)
                    ins = a.activation(out=lp[:], in_=big[:, 0:256], func=AF.Exp,
                                       scale=-1.0)
                elif k == 7:
                    W(a, "a", s_ffnp, 7 * qi + 6)
                    ins = a.copy(out=retf[:], in_=big[:, 0:E])
                ins.then_inc(s_ffne, 1)

            for qi in range(NQT + 2):
                qT = qi - 1
                qF = qi - 2
                if qi == 0:
                    a.dma_start(out=qf[0][:], in_=q_sc[0:128, :]
                                ).then_inc(s_qfs[0], 16)
                if qi < NQT:
                    act_head(qi, 0)
                    act_head(qi, 1)
                if 0 <= qT < NQT:
                    act_exp2(qT)
                Tq = [lambda tg=tg: _tre(qT, tg) for tg in range(TGs[qT])] \
                    if 0 <= qT < NQT else []
                xslot = {1: 2, 2: 4, 3: 5, 4: 5, 5: 6, 6: 7, 7: 8}
                ut = 0
                for h in range(H):
                    if qi < NQT and h >= 2:
                        act_head(qi, h)
                    if h == 2 and 0 <= qF < NQT:
                        _rstd(qF, 1)
                    if 0 <= qF < NQT:
                        for k in range(1, 8):
                            if xslot[k] == h:
                                _xe(qF, k)
                    if h >= 9 and ut < len(Tq):
                        Tq[ut]()
                        ut += 1
                    if h == 13 and 0 <= qF < NQT:
                        _rstd(qF, 2)
                    if h == 13 and qi + 1 < NQT:
                        W(a, "a", s_sc, cum_scg[qi])
                        a.dma_start(out=qf[(qi + 1) % 2][:],
                                    in_=q_sc[(qi + 1) * 128:(qi + 2) * 128, :]
                                    ).then_inc(s_qfs[(qi + 1) % 2], 16)
                    if h == 14 and qi < NQT:
                        if qi >= 2:
                            W(a, "a", s_ystr, 16 * (qi - 1))
                        a.dma_start(out=x0[qi % 2][:],
                                    in_=hs[qi * 128:(qi + 1) * 128, :]
                                    ).then_inc(s_x0s[qi % 2], 16)
                while ut < len(Tq):
                    Tq[ut]()
                    ut += 1

    return nc


_NC_CACHE = {}


def _get_nc():
    if "nc" not in _NC_CACHE:
        _NC_CACHE["nc"] = build_nc()
    return _NC_CACHE["nc"]


def _host_constants():
    import jax
    import jax.numpy as jnp

    with jax.default_device(jax.local_devices(backend="cpu")[0]):
        u = jax.random.uniform(jax.random.key(42, impl="threefry2x32"),
                               (8, T, T), jnp.float32, 1e-06, 1.0 - 1e-06)
        g2_all = np.asarray((-jnp.log(-jnp.log(u))) * jnp.float32(0.5),
                            dtype=np.float32)
    iidx = np.arange(T)
    g2_all = g2_all + np.where(iidx[None, :] < iidx[:, None], 0.0,
                               np.float32(NEG)).astype(np.float32)[None]

    smat = np.zeros((512, 256), np.float32)
    for leaf in range(256):
        for lvl in range(8):
            node = (1 << lvl) - 1 + (leaf >> (8 - lvl))
            bit = (leaf >> (7 - lvl)) & 1
            smat[node + 256 * bit, leaf] = 1.0
    idb = np.eye(128, dtype=np.float32).astype(BF)
    idf = np.eye(128, dtype=np.float32)
    col = np.arange(128)
    maskneg = np.where(col[None, :] < col[:, None], 0.0, NEG).astype(np.float32).astype(BF)
    return g2_all, smat, idb, idf, maskneg


def kernel(L0_attn_hidden_states, L0_attn_in_proj_weight, L0_attn_in_proj_bias,
           L0_ffn_Wq, L0_ffn_routers, L0_ffn_leaf_keys, L0_ffn_leaf_values,
           L0_ffn_Wout_w, L0_ffn_Wout_b):
    hs_all = np.ascontiguousarray(np.asarray(L0_attn_hidden_states, np.float32))
    W = np.asarray(L0_attn_in_proj_weight, np.float32)
    bias = np.asarray(L0_attn_in_proj_bias, np.float32)
    wqT = np.ascontiguousarray(W[:E].T)
    wkT = np.ascontiguousarray(W[E:2 * E].T)
    bqt = np.ascontiguousarray(bias[:E].reshape(8, 128).T)
    bkt = np.ascontiguousarray(bias[E:2 * E].reshape(8, 128).T)
    wqfT = np.ascontiguousarray(np.asarray(L0_ffn_Wq, np.float32).T)
    rT = np.ascontiguousarray(np.asarray(L0_ffn_routers, np.float32).T)
    lv = np.ascontiguousarray(np.asarray(L0_ffn_leaf_values, np.float32))
    woT = np.ascontiguousarray(np.asarray(L0_ffn_Wout_w, np.float32).T)
    wob = np.ascontiguousarray(np.asarray(L0_ffn_Wout_b, np.float32).reshape(1, E))

    g2_all, smat, idb, idf, maskneg = _host_constants()

    nc = _get_nc()
    in_maps = []
    for c in range(8):
        in_maps.append(dict(
            hs=np.ascontiguousarray(hs_all[c]), g2=g2_all[c],
            wqT=wqT, wkT=wkT, bqt=bqt, bkt=bkt, wqfT=wqfT, rT=rT, lv=lv,
            woT=woT, wob=wob, smat=smat, idb=idb, idf=idf, maskneg=maskneg,
        ))
    res = run_bass_kernel_spmd(nc, in_maps, core_ids=list(range(8)))
    out = np.stack([np.asarray(res.results[c]["y"], np.float32)
                    for c in range(8)], axis=0)
    return out
